# version 24
# baseline (speedup 1.0000x reference)
import numpy as np
from contextlib import ExitStack

import concourse.bass as bass
import concourse.mybir as mybir
from concourse.bass_utils import run_bass_kernel_spmd

F32 = mybir.dt.float32
BF16 = mybir.dt.bfloat16
FP8 = mybir.dt.float8e4
AF = mybir.ActivationFunctionType
ALU = mybir.AluOpType

P = 128
D = 1024
H = 4096
TOK = 4096
TT = 512
NT = TOK // TT
GD = D // P
GH = H // P
EPS = 1e-5
NW = float(D * H)

C_WAB, C_WCB, C_D, C_LNW, C_LNB, C_L1G, C_L1B, C_B2 = 0, 8, 16, 24, 32, 40, 48, 56
C_B1, C_L2G, C_L2B = 64, 96, 128
NCV = 160

ENGS = ("sync", "tensor", "vector", "scalar", "gpsimd")
SEMS = ("ld0", "ld1", "ld2", "ld3", "st0", "st1", "pe", "dv", "ac", "gp")

_DEBUG_TAPS = set()


def build_program():
    nc = bass.Bass("TRN2", target_bir_lowering=False, detect_race_conditions=False)

    xh = nc.dram_tensor("x", [D, TOK], F32, kind="ExternalInput")
    waT = nc.dram_tensor("waT", [D, D], F32, kind="ExternalInput")
    wbT = nc.dram_tensor("wbT", [D, D], F32, kind="ExternalInput")
    wcT = nc.dram_tensor("wcT", [D, D], F32, kind="ExternalInput")
    w1T = nc.dram_tensor("w1T", [D, H], F32, kind="ExternalInput")
    w2T = nc.dram_tensor("w2T", [H, D], F32, kind="ExternalInput")
    cvec = nc.dram_tensor("cvec", [P, NCV], F32, kind="ExternalInput")
    brow = nc.dram_tensor("brow", [33, D], F32, kind="ExternalInput")
    outh = nc.dram_tensor("out", [D, TOK], F32, kind="ExternalOutput")
    wcs = nc.dram_tensor("wcs", [P, 8192], BF16)

    plan = {e: [] for e in ENGS}
    cnt = {s: 0 for s in SEMS}

    dma_i = {"ld": 0, "st": 0}
    NLANE = {"ld": 4, "st": 2}

    def op(eng, fn, inc=None, waits=()):
        if inc in ("ld", "st"):
            i = dma_i[inc]
            dma_i[inc] = i + 1
            lane = "%s%d" % (inc, i % NLANE[inc])
            val = 16 * (i // NLANE[inc] + 1)
            w = tuple(waits) + (((lane, val - 16),) if val > 16 else ())
            plan[eng].append((fn, lane, w))
            cnt[lane] = val
            return (lane, val)
        plan[eng].append((fn, inc, tuple(waits)))
        if inc is not None:
            cnt[inc] += 1
            return cnt[inc]
        return None

    with ExitStack() as ctx:
        sb = lambda name, shape, dt: ctx.enter_context(nc.sbuf_tensor(name, shape, dt))

        def tap(name, ap_fn, shape, dtype):
            if name not in _DEBUG_TAPS:
                return
            th = nc.dram_tensor("dbg_" + name, shape, dtype, kind="ExternalOutput")
            waits = [(s, cnt[s]) for s in ("pe", "dv", "ac", "gp")]
            stc = op("sync", lambda e, th=th, ap_fn=ap_fn: e.dma_start(out=th[:, :], in_=ap_fn()),
                     inc="st", waits=waits)
            op("vector", lambda e: e.engine_nop(), waits=[stc])

        WAB = sb("WAB", [P, 8192], BF16)
        WBB = sb("WBB", [P, 8192], BF16)
        W1Q = sb("W1Q", [P, 32768], FP8)
        W2Q = sb("W2Q", [P, 32768], FP8)
        FA = sb("FA", [P, 4096], F32)
        FB = sb("FB", [P, 4096], F32)
        FC = sb("FC", [P, 4096], F32)
        BFP = sb("BFP", [P, 20480], BF16)
        SCR = sb("SCR", [P, 2048], BF16)
        ASL = sb("ASL", [P, 1024], F32)
        M1S = sb("M1S", [P, TT], F32)
        MCS = sb("MCS", [P, TT], F32)
        RCS = sb("RCS", [P, TT], F32)
        TMS = sb("TMS", [P, TT], F32)
        MRS = sb("MRS", [P, TT], F32)
        CV = sb("CV", [P, NCV], F32)
        DRV = sb("DRV", [P, 8], F32)
        QP = sb("QP", [P, 48], F32)
        KK = sb("KK", [P, 8], F32)
        LL = sb("LL", [P, 8], F32)
        CVB = sb("CVB", [P, 64], BF16)
        ONES = sb("ONESB", [P, P], BF16)
        ONESF = sb("ONESF", [P, P], F32)
        UBT = sb("UBT", [33, D], BF16)
        ONESR = sb("ONESR", [33, TT], BF16)
        CAR = sb("CAR", [P, 8], F32)
        EPSC = sb("EPSC", [P, 4], F32)
        PB = [ctx.enter_context(nc.psum_tensor("PB%d" % i, [P, TT], F32)) for i in range(8)]
        sem = {s: ctx.enter_context(nc.semaphore("s_" + s)) for s in SEMS}
        block = ctx.enter_context(nc.Block())

        WC = BFP
        stage = [FA, FB, FC]

        def hb(g):
            return BFP[:, 8192 + g * TT: 8192 + (g + 1) * TT]

        def xn(g):
            return BFP[:, 16384 + g * TT: 16384 + (g + 1) * TT]

        def gg(m):
            return BFP[:, m * TT: (m + 1) * TT]

        def scrA(i):
            return SCR[:, (i % 2) * TT: (i % 2 + 1) * TT]

        def scrB(i):
            return SCR[:, (2 + i % 2) * TT: (3 + i % 2) * TT]

        def fcs(g):
            return FC[:, g * TT:(g + 1) * TT]

        ld_cv0 = op("sync", lambda e: e.dma_start(out=CV[:, :], in_=cvec[:, :]), inc="ld")
        ld_cv = op("sync", lambda e: e.dma_start(out=FA[0:33, 0:D], in_=brow[:, :]), inc="ld")

        op("vector", lambda e: e.memset(ONES[:, :], 1.0))
        op("vector", lambda e: e.memset(ONESF[:, :], 1.0))
        op("vector", lambda e: e.memset(ONESR[:, :], 1.0))
        op("vector", lambda e: e.memset(CAR[:, :], 0.0))
        op("vector", lambda e: e.memset(EPSC[:, 0:1], EPS))
        op("vector", lambda e: e.memset(EPSC[:, 1:2], EPS * EPS))
        op("vector", lambda e: e.memset(EPSC[:, 2:3], 0.5))
        op("vector", lambda e: e.memset(EPSC[:, 3:4], -0.5))
        ubt_dv = op("vector", lambda e: e.tensor_copy(out=UBT[:, :], in_=FA[0:33, 0:D]),
                    inc="dv", waits=[ld_cv0, ld_cv])
        for j in range(GH):
            op("vector", lambda e, j=j: e.tensor_copy(out=CVB[:, 2 * j:2 * j + 1],
                                                      in_=CV[:, C_L2G + j:C_L2G + j + 1]))
            op("vector", lambda e, j=j: e.tensor_copy(out=CVB[:, 2 * j + 1:2 * j + 2],
                                                      in_=CV[:, C_L2B + j:C_L2B + j + 1]))

        ldx = [None] * (NT + 1)
        ldwc = [None] * (NT + 1)
        ldx[0] = op("sync", lambda e: e.dma_start(
            out=FB[:, :],
            in_=xh.ap().rearrange("(g p) t -> p g t", p=P)[:, :, 0:TT]),
            inc="ld")

        cast_done = []
        srcs = [(waT, 0, WAB, 0), (waT, 1, WAB, 4096), (wbT, 0, WBB, 0),
                (wbT, 1, WBB, 4096), (wcT, 0, WC, 0), (wcT, 1, WC, 4096)]
        stage2 = [FA, FC]
        for ci, (src, half, dstw, dco) in enumerate(srcs):
            stg = stage2[ci % 2]
            w = [("dv", cast_done[ci - 2])] if ci >= 2 else [("dv", ubt_dv)]
            ldc = op("sync", lambda e, src=src, half=half, stg=stg:
                     e.dma_start(out=stg[:, :],
                                 in_=src.ap().rearrange("(g p) n -> p g n", p=P)[:, 4 * half:4 * half + 4, :]),
                     inc="ld", waits=w)
            c = op("vector", lambda e, stg=stg, dstw=dstw, dco=dco:
                   e.tensor_copy(out=dstw[:, dco:dco + 4096], in_=stg[:, :]),
                   inc="dv", waits=[ldc])
            cast_done.append(c)
        wc_ready_dv = cast_done[-1]
        wcs_st = op("sync", lambda e: e.dma_start(out=wcs[:, :], in_=WC[:, 0:8192]),
                    inc="st", waits=[("dv", wc_ready_dv)])

        def w_half_ap(srcT, h):
            k, p2 = h // 2, h % 2
            if srcT is w1T:
                return w1T.ap()[k * P:(k + 1) * P, p2 * 2048:(p2 + 1) * 2048]
            return w2T.ap().rearrange("(j p) d -> p j d", p=P)[
                :, 4 * k + 2 * p2: 4 * k + 2 * p2 + 2, :]

        def fah(h):
            return FA[:, (h % 2) * 2048:(h % 2 + 1) * 2048]

        def T1H():
            return BFP[:, 16384:18432]

        def T2H():
            return BFP[:, 18432:20480]

        def q_p1(st, srcT, qp0, lo, hi, first_w=(), aw0=(), quarters=False):
            for h in range(lo, hi):
                if quarters:
                    w = list(first_w) if h < 2 else [("ac", st["abs"][2 * (h - 2) + 1])]
                else:
                    w = list(first_w) if h < 2 else [("ac", st["abs"][h - 2])]
                ldc = op("sync", lambda e, h=h, srcT=srcT: e.dma_start(
                    out=fah(h), in_=w_half_ap(srcT, h)), inc="ld", waits=w)
                aw = [ldc] + (list(aw0) if h == lo and lo == 0 else [])
                if quarters:
                    for q in range(2):
                        a = op("scalar", lambda e, h=h, q=q, qp0=qp0: e.activation(
                            out=ASL[:, 0:1024], in_=fah(h)[:, q * 1024:(q + 1) * 1024],
                            func=AF.Abs,
                            accum_out=QP[:, qp0 + 2 * h + q:qp0 + 2 * h + q + 1]),
                            inc="ac", waits=aw if q == 0 else [])
                        st["abs"].append(a)
                else:
                    a = op("scalar", lambda e, h=h, qp0=qp0: e.activation(
                        out=T1H(), in_=fah(h), func=AF.Abs,
                        accum_out=QP[:, qp0 + h:qp0 + h + 1]), inc="ac", waits=aw)
                    st["abs"].append(a)

        def q_stats(st, qp0, rcol, drv_sc, drv_sv, pm_bank, ncols=16):
            r = op("vector", lambda e, qp0=qp0, rcol=rcol, ncols=ncols: e.reduce_sum(
                out=DRV[:, rcol:rcol + 1], in_=QP[:, qp0:qp0 + ncols],
                axis=mybir.AxisListType.X),
                inc="dv", waits=[("ac", st["abs"][-1])])
            pm = op("tensor", lambda e, rcol=rcol, pm_bank=pm_bank: e.matmul(
                pm_bank[:, 0:1], ONESF[:, :], DRV[:, rcol:rcol + 1],
                start=True, stop=True), inc="pe", waits=[("dv", r)])
            c0 = op("vector", lambda e, pm_bank=pm_bank: e.tensor_scalar(
                out=DRV[:, 4:5], in0=pm_bank[:, 0:1],
                scalar1=1.0 / NW, scalar2=1e-5,
                op0=ALU.mult, op1=ALU.max), inc="dv", waits=[("pe", pm)])
            c1 = op("vector", lambda e, drv_sc=drv_sc: e.reciprocal(
                out=DRV[:, drv_sc:drv_sc + 1], in_=DRV[:, 4:5]),
                inc="dv", waits=[("dv", c0)])
            rv = op("vector", lambda e, drv_sc=drv_sc, drv_sv=drv_sv: e.reciprocal(
                out=DRV[:, drv_sv:drv_sv + 1], in_=DRV[:, drv_sc:drv_sc + 1]),
                inc="dv", waits=[("dv", c1)])
            hv = op("vector", lambda e, drv_sv=drv_sv: e.tensor_scalar(
                out=DRV[:, drv_sv:drv_sv + 1], in0=DRV[:, drv_sv:drv_sv + 1],
                scalar1=0.5, scalar2=None, op0=ALU.mult), inc="dv",
                waits=[("dv", rv)])
            return hv, c0

        def q_p2(st, srcT, qdst, drv_sc, lo, hi, rvw, t1f, t2f,
                 stage_last=(14, 15), first_w=()):
            for h in range(lo, hi):
                w = []
                if h < 2:
                    w += [("dv", rvw), ("ac", st["abs"][stage_last[h]])]
                else:
                    w.append(("ac", st["sg2"][h - 2]))
                ldc = op("sync", lambda e, h=h, srcT=srcT: e.dma_start(
                    out=fah(h), in_=w_half_ap(srcT, h)), inc="ld", waits=w)
                s1w = [ldc]
                if st["add"]:
                    s1w.append(("gp", st["add"][-1]))
                if h == 0:
                    s1w += [("dv", rvw)] + list(first_w)
                op("scalar", lambda e, h=h, drv_sc=drv_sc, t1f=t1f: e.activation(
                    out=t1f(), in_=fah(h), func=AF.Sign,
                    scale=DRV[:, drv_sc:drv_sc + 1], bias=EPSC[:, 3:4]),
                    waits=s1w)
                s2 = op("scalar", lambda e, h=h, drv_sc=drv_sc, t2f=t2f: e.activation(
                    out=t2f(), in_=fah(h), func=AF.Sign,
                    scale=DRV[:, drv_sc:drv_sc + 1], bias=EPSC[:, 2:3]),
                    inc="ac")
                ad = op("gpsimd", lambda e, h=h, qdst=qdst, t1f=t1f, t2f=t2f:
                        e.tensor_tensor(
                    out=qdst[:, (h // 2) * 4096 + (h % 2) * 2048:
                             (h // 2) * 4096 + (h % 2) * 2048 + 2048],
                    in0=t1f(), in1=t2f(), op=ALU.add), inc="gp",
                    waits=[("ac", s2)])
                st["sg2"].append(s2)
                st["add"].append(ad)

        def HB1():
            return BFP[:, 8192:10240]

        def HB2():
            return BFP[:, 10240:12288]

        q1 = {"abs": [], "sg2": [], "add": []}
        q2 = {"abs": [], "sg2": [], "add": []}

        def plan_kl(add_w2_last):
            nv = op("vector", lambda e: e.tensor_scalar(
                out=DRV[:, 5:6], in0=DRV[:, 3:4],
                scalar1=-1.0, scalar2=None, op0=ALU.mult), inc="dv")
            kl_done = []
            for m in range(GD):
                bank = PB[6 + m % 2]

                def mmseq(e, m=m, bank=bank):
                    last = None
                    for j in range(GH):
                        last = e.matmul(bank[:, 0:2],
                                        W2Q[:, j * 1024 + m * P: j * 1024 + (m + 1) * P],
                                        CVB[:, 2 * j:2 * j + 2],
                                        start=(j == 0), stop=(j == GH - 1))
                    return last
                w = [("gp", add_w2_last)] if m < 2 else [("dv", kl_done[m - 2])]
                pm = op("tensor", mmseq, inc="pe", waits=w)
                op("vector", lambda e, m=m, bank=bank: e.tensor_scalar(
                    out=KK[:, m:m + 1], in0=bank[:, 0:1],
                    scalar1=DRV[:, 5:6], scalar2=None, op0=ALU.mult),
                   waits=[("pe", pm), ("dv", nv)])
                c = op("vector", lambda e, m=m, bank=bank: e.scalar_tensor_tensor(
                    out=LL[:, m:m + 1], in0=bank[:, 1:2], scalar=DRV[:, 3:4],
                    in1=CV[:, C_B2 + m:C_B2 + m + 1], op0=ALU.mult, op1=ALU.add),
                    inc="dv")
                kl_done.append(c)
            return kl_done

        ONE_D = 1.0 / D
        ONE_H = 1.0 / H

        st_store = [("st0", 0)]
        sums45_dv = [0]
        sums67_dv = [0]
        scr_pe = [0, 0]
        carries_dv = [0]
        klh = [None]
        abc = {}

        def stats_single(b_mu, b_sq, inv, mu, r, t1, with_mr):
            op("vector", lambda e: e.tensor_scalar(out=mu[:, :], in0=b_mu[:, :],
                                                   scalar1=inv, scalar2=None, op0=ALU.mult))
            op("vector", lambda e: e.tensor_tensor(out=r[:, :], in0=mu[:, :],
                                                   in1=mu[:, :], op=ALU.mult))
            c = op("vector", lambda e: e.scalar_tensor_tensor(
                out=t1[:, :], in0=b_sq[:, :], scalar=inv, in1=r[:, :],
                op0=ALU.mult, op1=ALU.subtract), inc="dv")
            rv = c
            op("scalar", lambda e: e.activation(out=t1[:, :], in_=t1[:, :],
                                                func=AF.Ln, bias=EPSC[:, 0:1]),
               waits=[("dv", rv)])
            a1 = op("scalar", lambda e: e.activation(out=r[:, :], in_=t1[:, :],
                                                     func=AF.Exp, scale=-0.5), inc="ac")
            ready = [("ac", a1)]
            if with_mr is not None:
                mr = op("vector", lambda e: e.tensor_tensor(out=MRS[:, :], in0=mu[:, :],
                                                            in1=r[:, :], op=ALU.mult),
                        inc="dv", waits=[("ac", a1)] + with_mr)
                ready = [("ac", a1), ("dv", mr)]
            return c, ready

        def stats_double(b_mu, b_sq, inv, mu, r, t1, t2, skip_mu):
            if not skip_mu:
                op("vector", lambda e: e.tensor_scalar(out=mu[:, :], in0=b_mu[:, :],
                                                       scalar1=inv, scalar2=None, op0=ALU.mult))
            op("vector", lambda e: e.tensor_tensor(out=t2[:, :], in0=mu[:, :],
                                                   in1=mu[:, :], op=ALU.mult))
            c = op("vector", lambda e: e.scalar_tensor_tensor(
                out=t1[:, :], in0=b_sq[:, :], scalar=inv, in1=t2[:, :],
                op0=ALU.mult, op1=ALU.subtract), inc="dv")
            pr = c
            op("scalar", lambda e: e.activation(out=t1[:, :], in_=t1[:, :],
                                                func=AF.Ln, scale=1.0 + EPS,
                                                bias=EPSC[:, 1:2]), waits=[("dv", pr)])
            a1 = op("scalar", lambda e, r=r: e.activation(out=r[:, :], in_=t1[:, :],
                                                          func=AF.Exp, scale=-0.5),
                    inc="ac")
            return c, [("ac", a1)]

        def norm_chain(Xb, gcol, bcol, Mslot, Rslot, ready, extra0):
            last = None
            first = [True]
            pc_ac = {}
            for g in range(GD):
                w = (list(ready) + extra0) if first[0] else []
                first[0] = False
                bk0, bk1 = PB[g % 2], PB[2 + g % 2]
                if g >= 2:
                    w.append(("ac", pc_ac[g - 2]))
                op("vector", lambda e, g=g, Xb=Xb, Mslot=Mslot, bk0=bk0: e.tensor_tensor(
                    out=bk0[:, :], in0=Xb[:, g * TT:(g + 1) * TT], in1=Mslot[:, :],
                    op=ALU.subtract), waits=w)
                b = op("vector", lambda e, bk0=bk0, bk1=bk1, Rslot=Rslot: e.tensor_tensor(
                    out=bk1[:, :], in0=bk0[:, :], in1=Rslot[:, :], op=ALU.mult),
                    inc="dv")
                last = op("scalar", lambda e, g=g, gcol=gcol, bcol=bcol, bk1=bk1:
                          e.activation(
                    out=xn(g), in_=bk1[:, :], func=AF.Identity,
                    scale=CV[:, gcol + g:gcol + g + 1], bias=CV[:, bcol + g:bcol + g + 1]),
                    inc="ac", waits=[("dv", b)])
                pc_ac[g] = last
            return last

        def resid_chain(Xb, bias_col_base, bias_t, ready, out_to_fc):
            done = []
            first = {"vector": True, "gpsimd": True}
            firstdv = [True]
            for g in range(GD):
                bcol = (bias_t[:, g:g + 1] if bias_t is not None
                        else CV[:, bias_col_base + g:bias_col_base + g + 1])
                w = list(ready) if firstdv[0] else []
                firstdv[0] = False
                p1 = op("vector", lambda e, g=g, bcol=bcol: e.scalar_tensor_tensor(
                    out=fcs(g), in0=fcs(g), scalar=bcol, in1=MCS[:, :],
                    op0=ALU.add, op1=ALU.subtract), inc="dv", waits=w)
                eng = "gpsimd" if g % 2 == 0 else "vector"
                w2 = [("dv", p1)] if eng == "gpsimd" else []
                if first[eng]:
                    w2 = w2 + list(ready)
                    first[eng] = False
                op(eng, lambda e, g=g: e.tensor_tensor(
                    out=fcs(g), in0=fcs(g), in1=RCS[:, :], op=ALU.mult), waits=w2)
                if out_to_fc:
                    p3 = op(eng, lambda e, g=g, Xb=Xb: e.tensor_tensor(
                        out=fcs(g), in0=Xb[:, g * TT:(g + 1) * TT], in1=fcs(g), op=ALU.add),
                        inc="gp" if eng == "gpsimd" else "dv")
                else:
                    p3 = op(eng, lambda e, g=g, Xb=Xb: e.tensor_tensor(
                        out=Xb[:, g * TT:(g + 1) * TT], in0=Xb[:, g * TT:(g + 1) * TT],
                        in1=fcs(g), op=ALU.add),
                        inc="gp" if eng == "gpsimd" else "dv")
                done.append((("gp" if eng == "gpsimd" else "dv"), p3))
            return done

        def abc_sums(tt):
            Xb = FB if tt % 2 == 0 else FA
            s1_mm, s2_mm = [], []
            for g in range(GD):
                w = [ldx[tt], ("pe", s1_mm[g - 2] if g >= 2 else scr_pe[0])]
                c = op("vector", lambda e, g=g, Xb=Xb: e.tensor_copy(
                    out=scrA(g), in_=Xb[:, g * TT:(g + 1) * TT]), inc="dv", waits=w)
                w = [ldx[tt], ("pe", s2_mm[g - 2] if g >= 2 else scr_pe[1])]
                a = op("scalar", lambda e, g=g, Xb=Xb: e.activation(
                    out=scrB(g), in_=Xb[:, g * TT:(g + 1) * TT], func=AF.Square),
                    inc="ac", waits=w)
                m1 = op("tensor", lambda e, g=g: e.matmul(PB[4][:, :], ONES[:, :], scrA(g),
                                                          start=(g == 0), stop=(g == GD - 1)),
                        inc="pe", waits=[("dv", c), ("dv", sums45_dv[0])])
                s1_mm.append(m1)
                m2 = op("tensor", lambda e, g=g: e.matmul(PB[5][:, :], ONES[:, :], scrB(g),
                                                          start=(g == 0), stop=(g == GD - 1)),
                        inc="pe", waits=[("ac", a), ("dv", sums45_dv[0])])
                s2_mm.append(m2)
            scr_pe[0] = s1_mm[-1]
            scr_pe[1] = s2_mm[-1]
            return s2_mm[-1]

        def abc_stats(tt, s2last, t2slot=None):
            op("vector", lambda e: e.engine_nop(),
               waits=[("pe", s2last), ("gp", cnt["gp"]), ("ac", cnt["ac"])])
            sums_c1, ready1 = stats_double(PB[4], PB[5], ONE_D, M1S, MRS, TMS,
                                           t2slot if t2slot is not None else MCS, False)
            sums45_dv[0] = sums_c1
            return ready1

        def abc_chain(tt, ready1, extra0):
            Xb = FB if tt % 2 == 0 else FA
            first = [True]
            last = None
            for g in range(GD):
                w = (list(ready1) + extra0) if first[0] else []
                first[0] = False
                op("gpsimd", lambda e, g=g, Xb=Xb: e.tensor_tensor(
                    out=xn(g), in0=Xb[:, g * TT:(g + 1) * TT], in1=M1S[:, :],
                    op=ALU.subtract), waits=w)
                b = op("gpsimd", lambda e, g=g: e.tensor_tensor(
                    out=xn(g), in0=xn(g), in1=MRS[:, :], op=ALU.mult), inc="gp")
                last = op("scalar", lambda e, g=g: e.activation(
                    out=xn(g), in_=xn(g), func=AF.Identity,
                    scale=CV[:, C_LNW + g:C_LNW + g + 1], bias=CV[:, C_LNB + g:C_LNB + g + 1]),
                    inc="ac", waits=[("gp", b)])
            abc[tt] = last
            return last

        _r0 = abc_stats(0, abc_sums(0))
        abc_chain(0, _r0, [])
        if "m1" in _DEBUG_TAPS:
            tap("m1", lambda: M1S[:, :], [P, TT], F32)
            tap("r1", lambda: MRS[:, :], [P, TT], F32)
            tap("xn", lambda: BFP[:, 16384:20480], [P, 4096], BF16)

        def act_prefetch(func):
            op("scalar", lambda e, func=func: e.activation(
                out=QP[:, 16:17], in_=EPSC[:, 0:1], func=func))

        for t in range(NT):
            X = FB if t % 2 == 0 else FA
            XO = FA if t % 2 == 0 else FB
            xn_dv = abc[t]

            aev, scans = [], []
            for g in range(GD):
                ab, ub = PB[(2 * g) % 6], PB[(2 * g + 1) % 6]

                def amm(e, g=g, ab=ab):
                    last = None
                    for k in range(GD):
                        last = e.matmul(ab[:, :], WAB[:, k * 1024 + g * P: k * 1024 + (g + 1) * P],
                                        xn(k), start=(k == 0), stop=(k == GD - 1))
                    return last
                w = [("ac", xn_dv)] if g == 0 else []
                if g == 0 and t == 0:
                    w.append(("dv", cast_done[1]))
                if g >= 3:
                    w.append(("ac", aev[g - 3]))
                pa = op("tensor", amm, inc="pe", waits=w)

                def umm(e, g=g, ub=ub):
                    for k in range(GD):
                        e.matmul(ub[:, :], WBB[:, k * 1024 + g * P: k * 1024 + (g + 1) * P],
                                 xn(k), start=(k == 0), stop=False)
                    return e.matmul(ub[:, :], UBT[0:1, g * P:(g + 1) * P], ONESR[0:1, :],
                                    start=False, stop=True)
                w = [("dv", scans[g - 3])] if g >= 3 else []
                if g == 0 and t == 0:
                    w.append(("dv", cast_done[3]))
                pu = op("tensor", umm, inc="pe", waits=w)

                w = [("pe", pa)]
                if g >= 2:
                    w.append(("dv", scans[g - 2]))
                a = op("scalar", lambda e, g=g, ab=ab: e.activation(
                    out=ASL[:, (g % 2) * TT:(g % 2 + 1) * TT], in_=ab[:, :], func=AF.Sigmoid,
                    bias=CV[:, C_WAB + g:C_WAB + g + 1]), inc="ac", waits=w)
                aev.append(a)

                ini = 0.0 if t == 0 else CAR[:, g:g + 1]
                w = [("ac", a), ("pe", pu)]
                if g == 0 and t > 0:
                    w.append(("dv", carries_dv[0]))
                s = op("vector", lambda e, g=g, ub=ub, ini=ini: e.tensor_tensor_scan(
                    out=hb(g), data0=ASL[:, (g % 2) * TT:(g % 2 + 1) * TT], data1=ub[:, :],
                    initial=ini, op0=ALU.mult, op1=ALU.add), inc="dv", waits=w)
                scans.append(s)

            act_prefetch(AF.Ln)
            if t == 0:
                q_p1(q1, w1T, 0, 0, 6, first_w=[("dv", cast_done[4])],
                     aw0=[("pe", cnt["pe"])])
            carries = []
            for g in range(GD):
                s = op("vector", lambda e, g=g: e.tensor_copy(out=CAR[:, g:g + 1],
                                                              in_=hb(g)[:, TT - 1:TT]),
                       inc="dv", waits=[("dv", scans[-1])] if g == 0 else [])
                carries.append(s)
            carries_dv[0] = carries[-1]

            if t == 0 and "hb" in _DEBUG_TAPS:
                tap("hb", lambda: BFP[:, 8192:12288], [P, 4096], BF16)
            if t + 1 < NT and t > 0:
                ldx[t + 1] = op("sync", lambda e, t=t, XO=XO: e.dma_start(
                    out=XO[:, :],
                    in_=xh.ap().rearrange("(g p) t -> p g t", p=P)[:, :, (t + 1) * TT:(t + 2) * TT]),
                    inc="ld", waits=[("dv", carries[-1])])

            ymm, y0_dv, sy1, sy2 = [], [], [], []
            for mi in range(GD + 3):
                if mi < GD:
                    m = mi
                    yb_ = PB[m % 4]

                    def ymmf(e, m=m, yb_=yb_):
                        for k in range(GD):
                            e.matmul(yb_[:, :],
                                     WC[:, k * 1024 + m * P: k * 1024 + (m + 1) * P],
                                     hb(k), start=(k == 0), stop=False)
                        return e.matmul(yb_[:, :], UBT[32:33, m * P:(m + 1) * P],
                                        ONESR[32:33, :], start=False, stop=True)
                    w = []
                    if m == 0:
                        w.append(("dv", scans[-1]))
                        if t == 0:
                            w.append(("dv", wc_ready_dv))
                        else:
                            w.append(ldwc[t])
                    y0w = [st_store[0]] if m == 0 else []
                    if m >= 4:
                        w.append(("dv", y0_dv[m - 4]))
                    pm_ = op("tensor", ymmf, inc="pe", waits=w)
                    ymm.append(pm_)
                    c = op("vector", lambda e, m=m, yb_=yb_, X=X: e.scalar_tensor_tensor(
                        out=fcs(m), in0=X[:, m * TT:(m + 1) * TT],
                        scalar=CV[:, C_D + m:C_D + m + 1], in1=yb_[:, :],
                        op0=ALU.mult, op1=ALU.add), inc="dv", waits=[("pe", pm_)] + y0w)
                    y0_dv.append(c)
                if mi >= 3:
                    j = mi - 3
                    c2 = op("scalar", lambda e, j=j: e.activation(
                        out=scrA(j), in_=fcs(j), func=AF.Copy), inc="ac",
                        waits=[("pe", scr_pe[0] if j < 2 else sy1[j - 2]),
                               ("dv", y0_dv[j])])
                    a2 = op("scalar", lambda e, j=j: e.activation(
                        out=scrB(j), in_=fcs(j), func=AF.Square), inc="ac",
                        waits=[("pe", scr_pe[1] if j < 2 else sy2[j - 2])])
                    m1 = op("tensor", lambda e, j=j: e.matmul(PB[6][:, :], ONES[:, :], scrA(j),
                                                              start=(j == 0), stop=(j == GD - 1)),
                            inc="pe", waits=[("ac", c2), ("dv", sums67_dv[0])])
                    sy1.append(m1)
                    m2 = op("tensor", lambda e, j=j: e.matmul(PB[7][:, :], ONES[:, :], scrB(j),
                                                              start=(j == 0), stop=(j == GD - 1)),
                            inc="pe", waits=[("ac", a2), ("dv", sums67_dv[0])])
                    sy2.append(m2)
            scr_pe[0] = sy1[-1]
            scr_pe[1] = sy2[-1]
            if t == 0:
                q_p1(q1, w1T, 0, 6, 11)

            if t == 0 and "y0" in _DEBUG_TAPS:
                tap("y0", lambda: FC[:, :], [P, 4096], F32)
            op("vector", lambda e: e.engine_nop(), waits=[("pe", sy2[-1])])
            sums_c7, ready7 = stats_single(PB[6], PB[7], ONE_D, MCS, RCS, TMS, None)
            sums67_dv[0] = sums_c7
            if t == 0:
                q_p1(q1, w1T, 0, 11, 16)

            x1_done = []
            firsts = [True]
            for g in range(GD):
                w = list(ready7) if firsts[0] else []
                firsts[0] = False
                op("vector", lambda e, g=g: e.tensor_tensor(
                    out=fcs(g), in0=fcs(g), in1=MCS[:, :], op=ALU.subtract), waits=w)
                op("vector", lambda e, g=g: e.tensor_tensor(
                    out=fcs(g), in0=fcs(g), in1=RCS[:, :], op=ALU.mult))
                p3 = op("vector", lambda e, g=g, X=X: e.tensor_tensor(
                    out=X[:, g * TT:(g + 1) * TT], in0=X[:, g * TT:(g + 1) * TT],
                    in1=fcs(g), op=ALU.add), inc="dv")
                x1_done.append(("dv", p3))

            if t == 0 and "x1" in _DEBUG_TAPS:
                tap("x1", lambda X=X: X[:, :], [P, 4096], F32)
            sx2 = []
            for g in range(GD):
                w = [x1_done[g], ("pe", scr_pe[1] if g < 2 else sx2[g - 2])]
                a = op("scalar", lambda e, g=g, X=X: e.activation(
                    out=scrB(g), in_=X[:, g * TT:(g + 1) * TT], func=AF.Square),
                    inc="ac", waits=w)
                m2 = op("tensor", lambda e, g=g: e.matmul(PB[7][:, :], ONES[:, :], scrB(g),
                                                          start=(g == 0), stop=(g == GD - 1)),
                        inc="pe", waits=[("ac", a), ("dv", sums_c7)])
                sx2.append(m2)
            scr_pe[1] = sx2[-1]
            op("vector", lambda e: e.engine_nop(), waits=[("pe", sx2[-1])])
            sums_c9, ready9 = stats_double(None, PB[7], ONE_D, M1S, RCS, TMS, MCS, True)
            sums67_dv[0] = sums_c9
            hv1 = c0_1 = None
            if t == 0:
                hv1, c0_1 = q_stats(q1, 0, 6, 0, 1, PB[6])

            xn2_dv = norm_chain(X, C_L1G, C_L1B, M1S, RCS, ready9, [])
            if t == 0:
                q_p2(q1, w1T, W1Q, 0, 0, 16, hv1, HB1, HB2,
                     first_w=[("pe", ymm[-1])])
            if t == 0 and "xn2" in _DEBUG_TAPS:
                tap("xn2", lambda: BFP[:, 16384:20480], [P, 4096], BF16)

            abc_r1 = None
            if t + 1 < NT and t > 0:
                s2last = abc_sums(t + 1)
                abc_r1 = abc_stats(t + 1, s2last)
            act_prefetch(AF.Gelu)

            mm1, gel, g2c, sg1, sg2, gg_dv = [], [], [], [], [], []
            for mi in range(GH + 2):
                if mi >= 2:
                    j = mi - 2
                    m1_ = op("tensor", lambda e, j=j: e.matmul(PB[6][:, :], ONES[:, :], gg(j),
                                                               start=(j == 0), stop=(j == GH - 1)),
                             inc="pe",
                             waits=[("ac", gel[j])] + ([("dv", sums_c9)] if j == 0 else [])
                                   + ([("dv", c0_1)] if j == 0 and t == 0 else []))
                    sg1.append(m1_)
                    m2_ = op("tensor", lambda e, j=j: e.matmul(PB[7][:, :], ONES[:, :], scrB(j),
                                                               start=(j == 0), stop=(j == GH - 1)),
                             inc="pe", waits=[("dv", g2c[j])])
                    sg2.append(m2_)
                    gc = op("scalar", lambda e, j=j: e.activation(
                        out=gg(j), in_=gg(j), func=AF.Identity,
                        scale=CV[:, C_L2G + j:C_L2G + j + 1]), inc="ac",
                        waits=[("pe", m1_)])
                    gg_dv.append(gc)
                if mi < GH:
                    m = mi
                    bk = PB[m % 4]

                    def m1f(e, m=m, bk=bk):
                        last = None
                        for k in range(GD):
                            last = e.matmul(bk[:, :],
                                            W1Q[:, k * 4096 + m * P: k * 4096 + (m + 1) * P],
                                            xn(k), start=(k == 0), stop=(k == GD - 1))
                        return last
                    w = [("ac", xn2_dv)] if m == 0 else []
                    if m == 0 and t == 0:
                        w += [("gp", q1["add"][15]), ("dv", hv1)]
                    if m >= 4:
                        w.append(("ac", gel[m - 4]))
                    pm_ = op("tensor", m1f, inc="pe", waits=w)
                    mm1.append(pm_)
                    a = op("scalar", lambda e, m=m, bk=bk: e.activation(
                        out=gg(m), in_=bk[:, :], func=AF.Gelu,
                        bias=CV[:, C_B1 + m:C_B1 + m + 1], scale=DRV[:, 1:2]), inc="ac",
                        waits=[("pe", pm_)])
                    gel.append(a)
                    w = [("ac", a), ("pe", scr_pe[1] if m < 2 else sg2[m - 2])]
                    c = op("vector", lambda e, m=m: e.tensor_tensor(out=scrB(m), in0=gg(m),
                                                                    in1=gg(m), op=ALU.mult),
                           inc="dv", waits=w)
                    g2c.append(c)
            scr_pe[1] = sg2[-1]
            act_prefetch(AF.Ln)
            hv2 = c0_2 = None
            if t == 0:
                q_p1(q2, w2T, 16, 0, 16, quarters=True,
                     first_w=[("ac", q1["sg2"][14]), ("ac", q1["sg2"][15])],
                     aw0=[("dv", scans[-1]), ("gp", q1["add"][15])])
                hv2, c0_2 = q_stats(q2, 16, 7, 2, 3, PB[4], ncols=32)

            if t == 0 and "g" in _DEBUG_TAPS:
                tap("g", lambda: BFP[:, 0:16384], [P, 16384], BF16)
            op("vector", lambda e: e.engine_nop(), waits=[("pe", sg2[-1])])
            sums_c12, ready12 = stats_single(PB[6], PB[7], ONE_H, MCS, RCS, TMS, None)
            sums67_dv[0] = sums_c12

            if t == 0:
                q_p2(q2, w2T, W2Q, 2, 0, 16, hv2, T1H, T2H,
                     stage_last=(29, 31), first_w=[("pe", mm1[-1])])
                ldx[1] = op("sync", lambda e: e.dma_start(
                    out=FA[:, :],
                    in_=xh.ap().rearrange("(g p) t -> p g t", p=P)[:, :, TT:2 * TT]),
                    inc="ld", waits=[("ac", q2["sg2"][15])])
            else:
                if t + 1 < NT:
                    abc_chain(t + 1, abc_r1, [("pe", mm1[-1])])
            mr12w_ = []

            def plan_mr12():
                mr = op("gpsimd", lambda e: e.tensor_tensor(out=MRS[:, :], in0=MCS[:, :],
                                                            in1=RCS[:, :], op=ALU.mult),
                        inc="gp", waits=list(ready12))
                mr12w_.append(("gp", mr))
            if t > 0:
                plan_mr12()

            mm2, e1_dv, e2_dv = [], [], []

            def plan_mm2_e1(m):
                bk = PB[m % 4]

                def m2f(e, m=m, bk=bk):
                    last = None
                    for j in range(GH):
                        last = e.matmul(bk[:, :], W2Q[:, j * 1024 + m * P: j * 1024 + (m + 1) * P],
                                        gg(j), start=(j == 0), stop=(j == GH - 1))
                    return last
                w = [("ac", gg_dv[-1])] if m == 0 else []
                if m == 0 and t == 0:
                    w.append(("gp", q2["add"][15]))
                if m >= 4:
                    w.append(("dv", e1_dv[m - 4]))
                pm_ = op("tensor", m2f, inc="pe", waits=w)
                mm2.append(pm_)
                w = [("pe", pm_)]
                if m == 0:
                    w = w + ready12
                    w += [("dv", hv2)] if t == 0 else [("dv", klh[0][-1])]
                e1c = op("vector", lambda e, m=m, bk=bk: e.scalar_tensor_tensor(
                    out=fcs(m), in0=bk[:, :], scalar=DRV[:, 3:4], in1=RCS[:, :],
                    op0=ALU.mult, op1=ALU.mult), inc="dv", waits=w)
                e1_dv.append(e1c)

            def plan_e2(m):
                w2_ = list(mr12w_) if m == 0 else []
                if m == 0 and t == 0:
                    w2_.append(("dv", klh[0][-1]))
                c = op("vector", lambda e, m=m: e.scalar_tensor_tensor(
                    out=fcs(m), in0=MRS[:, :], scalar=KK[:, m:m + 1], in1=fcs(m),
                    op0=ALU.mult, op1=ALU.add), inc="dv", waits=w2_)
                e2_dv.append(c)

            if t == 0:
                for m in range(GD):
                    plan_mm2_e1(m)
                klh[0] = plan_kl(q2["add"][15])
                sums45_dv[0] = c0_2
                s2last = abc_sums(1)
                abc_r1 = abc_stats(1, s2last, t2slot=ASL[:, 0:TT])
                abc_chain(1, abc_r1, [("pe", mm1[-1]), ("ac", q2["sg2"][15])])
                plan_mr12()
                for m in range(GD):
                    plan_e2(m)
            else:
                for m in range(GD):
                    plan_mm2_e1(m)
                    plan_e2(m)

            if t == 0 and "t2" in _DEBUG_TAPS:
                tap("t2", lambda: FC[:, :], [P, 4096], F32)
            st1, st2 = [], []
            for m in range(GD):
                c = op("scalar", lambda e, m=m: e.activation(
                    out=scrA(m), in_=fcs(m), func=AF.Identity, bias=LL[:, m:m + 1]), inc="ac",
                    waits=[("dv", e2_dv[m]), ("pe", scr_pe[0] if m < 2 else st1[m - 2])])
                a = op("scalar", lambda e, m=m: e.activation(
                    out=scrB(m), in_=fcs(m), func=AF.Square, bias=LL[:, m:m + 1]),
                    inc="ac",
                    waits=[("dv", e2_dv[m]), ("pe", scr_pe[1] if m < 2 else st2[m - 2])])
                m1_ = op("tensor", lambda e, m=m: e.matmul(PB[6][:, :], ONES[:, :], scrA(m),
                                                           start=(m == 0), stop=(m == GD - 1)),
                         inc="pe", waits=[("ac", c), ("dv", sums_c12)])
                st1.append(m1_)
                m2_ = op("tensor", lambda e, m=m: e.matmul(PB[7][:, :], ONES[:, :], scrB(m),
                                                           start=(m == 0), stop=(m == GD - 1)),
                         inc="pe", waits=[("ac", a), ("dv", sums_c12)])
                st2.append(m2_)
            scr_pe[0] = st1[-1]
            scr_pe[1] = st2[-1]

            op("vector", lambda e: e.engine_nop(), waits=[("pe", st2[-1])])
            sums_c14, ready14 = stats_single(PB[6], PB[7], ONE_D, MCS, RCS, TMS, None)
            sums67_dv[0] = sums_c14
            if t + 1 < NT:
                act_prefetch(AF.Sigmoid)

            out_done = []
            fdv = [True]
            fgp = [True]
            for g in range(GD):
                w = list(ready14) if fdv[0] else []
                fdv[0] = False
                p1 = op("vector", lambda e, g=g: e.scalar_tensor_tensor(
                    out=fcs(g), in0=fcs(g), scalar=LL[:, g:g + 1], in1=MCS[:, :],
                    op0=ALU.add, op1=ALU.subtract), inc="dv", waits=w)
                w2 = [("dv", p1)] + (list(ready14) if fgp[0] else [])
                fgp[0] = False
                op("gpsimd", lambda e, g=g: e.tensor_tensor(
                    out=fcs(g), in0=fcs(g), in1=RCS[:, :], op=ALU.mult), waits=w2)
                p3 = op("gpsimd", lambda e, g=g, X=X: e.tensor_tensor(
                    out=fcs(g), in0=X[:, g * TT:(g + 1) * TT], in1=fcs(g), op=ALU.add),
                    inc="gp")
                out_done.append(("gp", p3))

            so = op("sync", lambda e, t=t: e.dma_start(
                out=outh.ap().rearrange("(g p) t -> p g t", p=P)[:, :, t * TT:(t + 1) * TT],
                in_=FC[:, :]), inc="st",
                waits=[out_done[-1]])
            st_store[0] = so

            if t + 1 < NT:
                w_ldwc = [("pe", mm2[-1])] + ([wcs_st] if t == 0 else [])
                ldwc[t + 1] = op("sync", lambda e: e.dma_start(out=WC[:, 0:8192], in_=wcs[:, :]),
                                 inc="ld", waits=w_ldwc)

        tap("drv", lambda: DRV[:, :], [P, 8], F32)
        tap("kk", lambda: KK[:, :], [P, 8], F32)
        tap("ll", lambda: LL[:, :], [P, 8], F32)
        tap("w1q", lambda: W1Q[:, :], [P, 32768], FP8)
        tap("w2q", lambda: W2Q[:, :], [P, 32768], FP8)

        fin = tuple((ln, cnt[ln]) for ln in ("st0", "st1") if cnt.get(ln))
        plan["sync"].append((lambda e: e.nop(), None, fin))

        def emit(eng_name):
            def _(e):
                for fn, inc, waits in plan[eng_name]:
                    for (s, v) in waits:
                        e.wait_ge(sem[s], v)
                    instr = fn(e)
                    if inc is not None:
                        instr.then_inc(sem[inc], 16 if inc.startswith(("ld", "st")) else 1)
            return _

        block.sync(emit("sync"))
        block.tensor(emit("tensor"))
        block.vector(emit("vector"))
        block.scalar(emit("scalar"))
        block.gpsimd(emit("gpsimd"))

    return nc


_CACHE = {}


def _pack_cv(d):
    cv = np.zeros((P, NCV), np.float32)

    def put(col, v, g):
        cv[:, col:col + g] = np.asarray(v, np.float32).reshape(g, P).T
    put(C_WAB, d["L0_attn_W_alpha_b"], 8)
    put(C_WCB, d["L0_attn_W_C_b"], 8)
    put(C_D, d["L0_attn_D_param"], 8)
    put(C_LNW, d["L0_attn_ln_w"], 8)
    put(C_LNB, d["L0_attn_ln_b"], 8)
    put(C_L1G, d["L0_ffn_ln1_g"], 8)
    put(C_L1B, d["L0_ffn_ln1_b"], 8)
    put(C_B2, d["L0_ffn_b2"], 8)
    put(C_B1, d["L0_ffn_b1"], 32)
    put(C_L2G, d["L0_ffn_ln2_g"], 32)
    put(C_L2B, d["L0_ffn_ln2_b"], 32)
    return cv


def kernel(**inputs):
    d = {k: np.asarray(v, dtype=np.float32) for k, v in inputs.items()}
    x = d["L0_attn_x"]
    waT = np.ascontiguousarray(d["L0_attn_W_alpha_w"].T)
    wbT = np.ascontiguousarray(d["L0_attn_W_B_w"].T)
    wcT = np.ascontiguousarray(d["L0_attn_W_C_w"].T)
    w1T = np.ascontiguousarray(d["L0_ffn_w1"].T)
    w2T = np.ascontiguousarray(d["L0_ffn_w2"].T)
    cv = _pack_cv(d)
    wbb = np.zeros((33, D), np.float32)
    wbb[0] = d["L0_attn_W_B_b"]
    wbb[32] = d["L0_attn_W_C_b"]

    if "nc" not in _CACHE:
        _CACHE["nc"] = build_program()
    nc = _CACHE["nc"]

    in_maps = [{
        "x": np.ascontiguousarray(x[c].T),
        "waT": waT, "wbT": wbT, "wcT": wcT, "w1T": w1T, "w2T": w2T,
        "cvec": cv, "brow": wbb,
    } for c in range(8)]
    res = run_bass_kernel_spmd(nc, in_maps, core_ids=list(range(8)))
    out = np.stack([np.asarray(r["out"]).T for r in res.results], axis=0)
    return out.astype(np.float32)


# version 25
# speedup vs baseline: 1.0683x; 1.0683x over previous
import numpy as np
from contextlib import ExitStack

import concourse.bass as bass
import concourse.mybir as mybir
from concourse.bass_utils import run_bass_kernel_spmd

F32 = mybir.dt.float32
BF16 = mybir.dt.bfloat16
FP8 = mybir.dt.float8e4
AF = mybir.ActivationFunctionType
ALU = mybir.AluOpType

P = 128
D = 1024
H = 4096
TOK = 4096
TT = 512
NT = TOK // TT
GD = D // P
GH = H // P
EPS = 1e-5
NW = float(D * H)

C_WAB, C_WCB, C_D, C_LNW, C_LNB, C_L1G, C_L1B, C_B2 = 0, 8, 16, 24, 32, 40, 48, 56
C_B1, C_L2G, C_L2B = 64, 96, 128
NCV = 160

ENGS = ("sync", "tensor", "vector", "scalar", "gpsimd")
SEMS = ("ld0", "ld1", "ld2", "ld3", "st0", "st1", "pe", "dv", "ac", "gp")

_DEBUG_TAPS = set()


def build_program():
    nc = bass.Bass("TRN2", target_bir_lowering=False, detect_race_conditions=False)

    xh = nc.dram_tensor("x", [D, TOK], F32, kind="ExternalInput")
    waT = nc.dram_tensor("waT", [D, D], F32, kind="ExternalInput")
    wbT = nc.dram_tensor("wbT", [D, D], F32, kind="ExternalInput")
    wcT = nc.dram_tensor("wcT", [D, D], F32, kind="ExternalInput")
    w1T = nc.dram_tensor("w1T", [D, H], F32, kind="ExternalInput")
    w2T = nc.dram_tensor("w2T", [H, D], F32, kind="ExternalInput")
    cvec = nc.dram_tensor("cvec", [P, NCV], F32, kind="ExternalInput")
    brow = nc.dram_tensor("brow", [33, D], F32, kind="ExternalInput")
    outh = nc.dram_tensor("out", [D, TOK], F32, kind="ExternalOutput")
    wcs = nc.dram_tensor("wcs", [P, 8192], BF16)

    plan = {e: [] for e in ENGS}
    cnt = {s: 0 for s in SEMS}

    dma_i = {"ld": 0, "st": 0}
    NLANE = {"ld": 4, "st": 2}

    def op(eng, fn, inc=None, waits=()):
        if inc in ("ld", "st"):
            i = dma_i[inc]
            dma_i[inc] = i + 1
            lane = "%s%d" % (inc, i % NLANE[inc])
            val = 16 * (i // NLANE[inc] + 1)
            w = tuple(waits) + (((lane, val - 16),) if val > 16 else ())
            plan[eng].append((fn, lane, w))
            cnt[lane] = val
            return (lane, val)
        plan[eng].append((fn, inc, tuple(waits)))
        if inc is not None:
            cnt[inc] += 1
            return cnt[inc]
        return None

    with ExitStack() as ctx:
        sb = lambda name, shape, dt: ctx.enter_context(nc.sbuf_tensor(name, shape, dt))

        def tap(name, ap_fn, shape, dtype):
            if name not in _DEBUG_TAPS:
                return
            th = nc.dram_tensor("dbg_" + name, shape, dtype, kind="ExternalOutput")
            waits = [(s, cnt[s]) for s in ("pe", "dv", "ac", "gp")]
            stc = op("sync", lambda e, th=th, ap_fn=ap_fn: e.dma_start(out=th[:, :], in_=ap_fn()),
                     inc="st", waits=waits)
            op("vector", lambda e: e.engine_nop(), waits=[stc])

        WAB = sb("WAB", [P, 8192], BF16)
        WBB = sb("WBB", [P, 8192], BF16)
        W1Q = sb("W1Q", [P, 32768], FP8)
        W2Q = sb("W2Q", [P, 32768], FP8)
        FA = sb("FA", [P, 4096], F32)
        FB = sb("FB", [P, 4096], F32)
        FC = sb("FC", [P, 4096], F32)
        BFP = sb("BFP", [P, 20480], BF16)
        SCR = sb("SCR", [P, 2048], BF16)
        ASL = sb("ASL", [P, 1024], F32)
        M1S = sb("M1S", [P, TT], F32)
        MCS = sb("MCS", [P, TT], F32)
        RCS = sb("RCS", [P, TT], F32)
        TMS = sb("TMS", [P, TT], F32)
        MRS = sb("MRS", [P, TT], F32)
        CV = sb("CV", [P, NCV], F32)
        DRV = sb("DRV", [P, 8], F32)
        QP = sb("QP", [P, 32], F32)
        KK = sb("KK", [P, 8], F32)
        LL = sb("LL", [P, 8], F32)
        CVB = sb("CVB", [P, 64], BF16)
        ONES = sb("ONESB", [P, P], BF16)
        ONESF = sb("ONESF", [P, P], F32)
        UBT = sb("UBT", [33, D], BF16)
        ONESR = sb("ONESR", [33, TT], BF16)
        CAR = sb("CAR", [P, 8], F32)
        EPSC = sb("EPSC", [P, 2], F32)
        PB = [ctx.enter_context(nc.psum_tensor("PB%d" % i, [P, TT], F32)) for i in range(8)]
        sem = {s: ctx.enter_context(nc.semaphore("s_" + s)) for s in SEMS}
        block = ctx.enter_context(nc.Block())

        WC = BFP
        stage = [FA, FB, FC]

        def hb(g):
            return BFP[:, 8192 + g * TT: 8192 + (g + 1) * TT]

        def xn(g):
            return BFP[:, 16384 + g * TT: 16384 + (g + 1) * TT]

        def gg(m):
            return BFP[:, m * TT: (m + 1) * TT]

        def scrA(i):
            return SCR[:, (i % 2) * TT: (i % 2 + 1) * TT]

        def scrB(i):
            return SCR[:, (2 + i % 2) * TT: (3 + i % 2) * TT]

        def fcs(g):
            return FC[:, g * TT:(g + 1) * TT]

        ld_cv0 = op("sync", lambda e: e.dma_start(out=CV[:, :], in_=cvec[:, :]), inc="ld")
        ld_cv = op("sync", lambda e: e.dma_start(out=FA[0:33, 0:D], in_=brow[:, :]), inc="ld")

        op("vector", lambda e: e.memset(ONES[:, :], 1.0))
        op("vector", lambda e: e.memset(ONESF[:, :], 1.0))
        op("vector", lambda e: e.memset(ONESR[:, :], 1.0))
        op("vector", lambda e: e.memset(CAR[:, :], 0.0))
        op("vector", lambda e: e.memset(EPSC[:, 0:1], EPS))
        op("vector", lambda e: e.memset(EPSC[:, 1:2], EPS * EPS))
        ubt_dv = op("vector", lambda e: e.tensor_copy(out=UBT[:, :], in_=FA[0:33, 0:D]),
                    inc="dv", waits=[ld_cv0, ld_cv])
        for j in range(GH):
            op("vector", lambda e, j=j: e.tensor_copy(out=CVB[:, 2 * j:2 * j + 1],
                                                      in_=CV[:, C_L2G + j:C_L2G + j + 1]))
            op("vector", lambda e, j=j: e.tensor_copy(out=CVB[:, 2 * j + 1:2 * j + 2],
                                                      in_=CV[:, C_L2B + j:C_L2B + j + 1]))

        cast_done = []
        srcs = [(waT, 0, WAB, 0), (waT, 1, WAB, 4096), (wbT, 0, WBB, 0),
                (wbT, 1, WBB, 4096), (wcT, 0, WC, 0), (wcT, 1, WC, 4096)]
        for ci, (src, half, dstw, dco) in enumerate(srcs):
            stg = stage[ci % 3]
            w = [("dv", cast_done[ci - 3])] if ci >= 3 else [("dv", ubt_dv)]
            ldc = op("sync", lambda e, src=src, half=half, stg=stg:
                     e.dma_start(out=stg[:, :],
                                 in_=src.ap().rearrange("(g p) n -> p g n", p=P)[:, 4 * half:4 * half + 4, :]),
                     inc="ld", waits=w)
            c = op("vector", lambda e, stg=stg, dstw=dstw, dco=dco:
                   e.tensor_copy(out=dstw[:, dco:dco + 4096], in_=stg[:, :]),
                   inc="dv", waits=[ldc])
            cast_done.append(c)
        wc_ready_dv = cast_done[-1]
        wcs_st = op("sync", lambda e: e.dma_start(out=wcs[:, :], in_=WC[:, 0:8192]),
                    inc="st", waits=[("dv", wc_ready_dv)])

        def w_chunk_ap(srcT, k):
            if srcT is w1T:
                return srcT.ap()[k * P:(k + 1) * P, :]
            return srcT.ap().rearrange("(j p) d -> p j d", p=P)[:, 4 * k:4 * k + 4, :]

        def quant_weight(srcT, qdst, qp0, drv_sc, drv_sv, stage_guard):
            acc_done = []
            for k in range(8):
                stg = stage[k % 3]
                w = list(stage_guard) if k < 3 else [("ac", acc_done[k - 3])]
                ldc = op("sync", lambda e, k=k, stg=stg, srcT=srcT:
                         e.dma_start(out=stg[:, :], in_=w_chunk_ap(srcT, k)),
                         inc="ld", waits=w)
                a = op("scalar", lambda e, stg=stg, k=k:
                       e.activation(out=BFP[:, 8192:12288], in_=stg[:, :], func=AF.Abs,
                                    accum_out=QP[:, qp0 + k:qp0 + k + 1]),
                       inc="ac", waits=[ldc])
                acc_done.append(a)
            r = op("vector", lambda e: e.reduce_sum(out=QP[:, 30:31], in_=QP[:, qp0:qp0 + 8],
                                                    axis=mybir.AxisListType.X),
                   inc="dv", waits=[("ac", acc_done[-1])])
            pm = op("tensor", lambda e: e.matmul(PB[6][:, 0:1], ONESF[:, :], QP[:, 30:31],
                                                 start=True, stop=True),
                    inc="pe", waits=[("dv", r)])
            c0 = op("vector", lambda e: e.tensor_scalar(out=DRV[:, 4:5], in0=PB[6][:, 0:1],
                                                        scalar1=1.0 / NW, scalar2=1e-5,
                                                        op0=ALU.mult, op1=ALU.max),
                    inc="dv", waits=[("pe", pm)])
            c1 = op("vector", lambda e: e.reciprocal(out=DRV[:, drv_sc:drv_sc + 1],
                                                     in_=DRV[:, 4:5]),
                    inc="dv", waits=[("dv", c0)])
            rv = op("vector", lambda e: e.reciprocal(out=DRV[:, drv_sv:drv_sv + 1],
                                                     in_=DRV[:, drv_sc:drv_sc + 1]),
                    inc="dv", waits=[("dv", c1)])
            tern_done = []
            for k in range(8):
                stg = stage[k % 3]
                w = [("dv", rv)] if k < 3 else [("dv", tern_done[k - 3])]
                ldc = op("sync", lambda e, k=k, stg=stg, srcT=srcT:
                         e.dma_start(out=stg[:, :], in_=w_chunk_ap(srcT, k)),
                         inc="ld", waits=w)
                op("vector", lambda e, stg=stg: e.tensor_scalar(
                    out=BFP[:, 8192:12288], in0=stg[:, :],
                    scalar1=DRV[:, drv_sc:drv_sc + 1], scalar2=0.5,
                    op0=ALU.mult, op1=ALU.is_gt), waits=[ldc, ("dv", rv)])
                op("vector", lambda e, stg=stg: e.tensor_scalar(
                    out=BFP[:, 12288:16384], in0=stg[:, :],
                    scalar1=DRV[:, drv_sc:drv_sc + 1], scalar2=-0.5,
                    op0=ALU.mult, op1=ALU.is_lt))
                c = op("vector", lambda e, k=k: e.tensor_tensor(
                    out=qdst[:, k * 4096:(k + 1) * 4096],
                    in0=BFP[:, 8192:12288], in1=BFP[:, 12288:16384], op=ALU.subtract),
                    inc="dv")
                tern_done.append(c)
            return tern_done

        g1 = [("dv", wc_ready_dv)] * 3
        t1d = quant_weight(w1T, W1Q, 0, 0, 1, g1)
        g2 = [("dv", t1d[5]), ("dv", t1d[6]), ("dv", t1d[7])]
        t2d = quant_weight(w2T, W2Q, 8, 2, 3, g2)
        q2_dv = t2d[-1]

        nv = op("vector", lambda e: e.tensor_scalar(out=DRV[:, 5:6], in0=DRV[:, 3:4],
                                                    scalar1=-1.0, scalar2=None, op0=ALU.mult),
                inc="dv")

        kl_done = []
        for m in range(GD):
            bank = PB[m % 2]

            def mmseq(e, m=m, bank=bank):
                last = None
                for j in range(GH):
                    last = e.matmul(bank[:, 0:2],
                                    W2Q[:, j * 1024 + m * P: j * 1024 + (m + 1) * P],
                                    CVB[:, 2 * j:2 * j + 2],
                                    start=(j == 0), stop=(j == GH - 1))
                return last
            w = [("dv", q2_dv)] if m < 2 else [("dv", kl_done[m - 2])]
            pm = op("tensor", mmseq, inc="pe", waits=w)
            op("vector", lambda e, m=m, bank=bank: e.tensor_scalar(
                out=KK[:, m:m + 1], in0=bank[:, 0:1],
                scalar1=DRV[:, 5:6], scalar2=None, op0=ALU.mult),
               waits=[("pe", pm), ("dv", nv)])
            c = op("vector", lambda e, m=m, bank=bank: e.scalar_tensor_tensor(
                out=LL[:, m:m + 1], in0=bank[:, 1:2], scalar=DRV[:, 3:4],
                in1=CV[:, C_B2 + m:C_B2 + m + 1], op0=ALU.mult, op1=ALU.add),
                inc="dv")
            kl_done.append(c)

        tap("drv", lambda: DRV[:, :], [P, 8], F32)
        tap("kk", lambda: KK[:, :], [P, 8], F32)
        tap("ll", lambda: LL[:, :], [P, 8], F32)
        tap("w1q", lambda: W1Q[:, :], [P, 32768], FP8)
        tap("w2q", lambda: W2Q[:, :], [P, 32768], FP8)

        ldx = [None] * (NT + 1)
        ldwc = [None] * (NT + 1)
        ldx[0] = op("sync", lambda e: e.dma_start(
            out=FA[:, :],
            in_=xh.ap().rearrange("(g p) t -> p g t", p=P)[:, :, 0:TT]),
            inc="ld", waits=[("dv", q2_dv)])

        ONE_D = 1.0 / D
        ONE_H = 1.0 / H

        st_store = [("st0", 0)]
        sums45_dv = [0]
        sums67_dv = [0]
        scr_pe = [0, 0]
        carries_dv = [0]
        abc = {}

        def stats_single(b_mu, b_sq, inv, mu, r, t1, with_mr):
            op("vector", lambda e: e.tensor_scalar(out=mu[:, :], in0=b_mu[:, :],
                                                   scalar1=inv, scalar2=None, op0=ALU.mult))
            op("vector", lambda e: e.tensor_tensor(out=r[:, :], in0=mu[:, :],
                                                   in1=mu[:, :], op=ALU.mult))
            c = op("vector", lambda e: e.scalar_tensor_tensor(
                out=t1[:, :], in0=b_sq[:, :], scalar=inv, in1=r[:, :],
                op0=ALU.mult, op1=ALU.subtract), inc="dv")
            rv = c
            op("scalar", lambda e: e.activation(out=t1[:, :], in_=t1[:, :],
                                                func=AF.Ln, bias=EPSC[:, 0:1]),
               waits=[("dv", rv)])
            a1 = op("scalar", lambda e: e.activation(out=r[:, :], in_=t1[:, :],
                                                     func=AF.Exp, scale=-0.5), inc="ac")
            ready = [("ac", a1)]
            if with_mr is not None:
                mr = op("vector", lambda e: e.tensor_tensor(out=MRS[:, :], in0=mu[:, :],
                                                            in1=r[:, :], op=ALU.mult),
                        inc="dv", waits=[("ac", a1)] + with_mr)
                ready = [("ac", a1), ("dv", mr)]
            return c, ready

        def stats_double(b_mu, b_sq, inv, mu, r, t1, t2, skip_mu):
            if not skip_mu:
                op("vector", lambda e: e.tensor_scalar(out=mu[:, :], in0=b_mu[:, :],
                                                       scalar1=inv, scalar2=None, op0=ALU.mult))
            op("vector", lambda e: e.tensor_tensor(out=t2[:, :], in0=mu[:, :],
                                                   in1=mu[:, :], op=ALU.mult))
            c = op("vector", lambda e: e.scalar_tensor_tensor(
                out=t1[:, :], in0=b_sq[:, :], scalar=inv, in1=t2[:, :],
                op0=ALU.mult, op1=ALU.subtract), inc="dv")
            pr = c
            op("scalar", lambda e: e.activation(out=t1[:, :], in_=t1[:, :],
                                                func=AF.Ln, scale=1.0 + EPS,
                                                bias=EPSC[:, 1:2]), waits=[("dv", pr)])
            a1 = op("scalar", lambda e, r=r: e.activation(out=r[:, :], in_=t1[:, :],
                                                          func=AF.Exp, scale=-0.5),
                    inc="ac")
            return c, [("ac", a1)]

        def norm_chain(Xb, gcol, bcol, Mslot, Rslot, ready, extra0):
            last = None
            first = [True]
            pc_ac = {}
            for g in range(GD):
                w = (list(ready) + extra0) if first[0] else []
                first[0] = False
                bk0, bk1 = PB[g % 2], PB[2 + g % 2]
                if g >= 2:
                    w.append(("ac", pc_ac[g - 2]))
                op("vector", lambda e, g=g, Xb=Xb, Mslot=Mslot, bk0=bk0: e.tensor_tensor(
                    out=bk0[:, :], in0=Xb[:, g * TT:(g + 1) * TT], in1=Mslot[:, :],
                    op=ALU.subtract), waits=w)
                b = op("vector", lambda e, bk0=bk0, bk1=bk1, Rslot=Rslot: e.tensor_tensor(
                    out=bk1[:, :], in0=bk0[:, :], in1=Rslot[:, :], op=ALU.mult),
                    inc="dv")
                last = op("scalar", lambda e, g=g, gcol=gcol, bcol=bcol, bk1=bk1:
                          e.activation(
                    out=xn(g), in_=bk1[:, :], func=AF.Identity,
                    scale=CV[:, gcol + g:gcol + g + 1], bias=CV[:, bcol + g:bcol + g + 1]),
                    inc="ac", waits=[("dv", b)])
                pc_ac[g] = last
            return last

        def resid_chain(Xb, bias_col_base, bias_t, ready, out_to_fc):
            done = []
            first = {"vector": True, "gpsimd": True}
            firstdv = [True]
            for g in range(GD):
                bcol = (bias_t[:, g:g + 1] if bias_t is not None
                        else CV[:, bias_col_base + g:bias_col_base + g + 1])
                w = list(ready) if firstdv[0] else []
                firstdv[0] = False
                p1 = op("vector", lambda e, g=g, bcol=bcol: e.scalar_tensor_tensor(
                    out=fcs(g), in0=fcs(g), scalar=bcol, in1=MCS[:, :],
                    op0=ALU.add, op1=ALU.subtract), inc="dv", waits=w)
                eng = "gpsimd" if g % 2 == 0 else "vector"
                w2 = [("dv", p1)] if eng == "gpsimd" else []
                if first[eng]:
                    w2 = w2 + list(ready)
                    first[eng] = False
                op(eng, lambda e, g=g: e.tensor_tensor(
                    out=fcs(g), in0=fcs(g), in1=RCS[:, :], op=ALU.mult), waits=w2)
                if out_to_fc:
                    p3 = op(eng, lambda e, g=g, Xb=Xb: e.tensor_tensor(
                        out=fcs(g), in0=Xb[:, g * TT:(g + 1) * TT], in1=fcs(g), op=ALU.add),
                        inc="gp" if eng == "gpsimd" else "dv")
                else:
                    p3 = op(eng, lambda e, g=g, Xb=Xb: e.tensor_tensor(
                        out=Xb[:, g * TT:(g + 1) * TT], in0=Xb[:, g * TT:(g + 1) * TT],
                        in1=fcs(g), op=ALU.add),
                        inc="gp" if eng == "gpsimd" else "dv")
                done.append((("gp" if eng == "gpsimd" else "dv"), p3))
            return done

        def abc_sums(tt):
            Xb = FA if tt % 2 == 0 else FB
            s1_mm, s2_mm = [], []
            for g in range(GD):
                w = [ldx[tt], ("pe", s1_mm[g - 2] if g >= 2 else scr_pe[0])]
                c = op("vector", lambda e, g=g, Xb=Xb: e.tensor_copy(
                    out=scrA(g), in_=Xb[:, g * TT:(g + 1) * TT]), inc="dv", waits=w)
                w = [ldx[tt], ("pe", s2_mm[g - 2] if g >= 2 else scr_pe[1])]
                a = op("scalar", lambda e, g=g, Xb=Xb: e.activation(
                    out=scrB(g), in_=Xb[:, g * TT:(g + 1) * TT], func=AF.Square),
                    inc="ac", waits=w)
                m1 = op("tensor", lambda e, g=g: e.matmul(PB[4][:, :], ONES[:, :], scrA(g),
                                                          start=(g == 0), stop=(g == GD - 1)),
                        inc="pe", waits=[("dv", c), ("dv", sums45_dv[0])])
                s1_mm.append(m1)
                m2 = op("tensor", lambda e, g=g: e.matmul(PB[5][:, :], ONES[:, :], scrB(g),
                                                          start=(g == 0), stop=(g == GD - 1)),
                        inc="pe", waits=[("ac", a), ("dv", sums45_dv[0])])
                s2_mm.append(m2)
            scr_pe[0] = s1_mm[-1]
            scr_pe[1] = s2_mm[-1]
            return s2_mm[-1]

        def abc_stats(tt, s2last):
            op("vector", lambda e: e.engine_nop(),
               waits=[("pe", s2last), ("gp", cnt["gp"]), ("ac", cnt["ac"])])
            sums_c1, ready1 = stats_double(PB[4], PB[5], ONE_D, M1S, MRS, TMS, MCS, False)
            sums45_dv[0] = sums_c1
            return ready1

        def abc_chain(tt, ready1, extra0):
            Xb = FA if tt % 2 == 0 else FB
            first = [True]
            last = None
            for g in range(GD):
                w = (list(ready1) + extra0) if first[0] else []
                first[0] = False
                op("gpsimd", lambda e, g=g, Xb=Xb: e.tensor_tensor(
                    out=xn(g), in0=Xb[:, g * TT:(g + 1) * TT], in1=M1S[:, :],
                    op=ALU.subtract), waits=w)
                b = op("gpsimd", lambda e, g=g: e.tensor_tensor(
                    out=xn(g), in0=xn(g), in1=MRS[:, :], op=ALU.mult), inc="gp")
                last = op("scalar", lambda e, g=g: e.activation(
                    out=xn(g), in_=xn(g), func=AF.Identity,
                    scale=CV[:, C_LNW + g:C_LNW + g + 1], bias=CV[:, C_LNB + g:C_LNB + g + 1]),
                    inc="ac", waits=[("gp", b)])
            abc[tt] = last
            return last

        _r0 = abc_stats(0, abc_sums(0))
        abc[0] = norm_chain(FA, C_LNW, C_LNB, M1S, MRS, _r0, [])
        op("scalar", lambda e: e.activation(out=QP[:, 16:17], in_=EPSC[:, 0:1],
                                            func=AF.Sigmoid))
        if "m1" in _DEBUG_TAPS:
            tap("m1", lambda: M1S[:, :], [P, TT], F32)
            tap("r1", lambda: MRS[:, :], [P, TT], F32)
            tap("xn", lambda: BFP[:, 16384:20480], [P, 4096], BF16)

        def act_prefetch(func):
            op("scalar", lambda e, func=func: e.activation(
                out=QP[:, 16:17], in_=EPSC[:, 0:1], func=func))

        for t in range(NT):
            X = FA if t % 2 == 0 else FB
            XO = FB if t % 2 == 0 else FA
            xn_dv = abc[t]

            aev, scans = [], []
            for g in range(GD):
                ab, ub = PB[(2 * g) % 6], PB[(2 * g + 1) % 6]

                def amm(e, g=g, ab=ab):
                    last = None
                    for k in range(GD):
                        last = e.matmul(ab[:, :], WAB[:, k * 1024 + g * P: k * 1024 + (g + 1) * P],
                                        xn(k), start=(k == 0), stop=(k == GD - 1))
                    return last
                w = [("ac", xn_dv)] if g == 0 else []
                if g >= 3:
                    w.append(("ac", aev[g - 3]))
                pa = op("tensor", amm, inc="pe", waits=w)

                def umm(e, g=g, ub=ub):
                    for k in range(GD):
                        e.matmul(ub[:, :], WBB[:, k * 1024 + g * P: k * 1024 + (g + 1) * P],
                                 xn(k), start=(k == 0), stop=False)
                    return e.matmul(ub[:, :], UBT[0:1, g * P:(g + 1) * P], ONESR[0:1, :],
                                    start=False, stop=True)
                w = [("dv", scans[g - 3])] if g >= 3 else []
                pu = op("tensor", umm, inc="pe", waits=w)

                w = [("pe", pa)]
                if g >= 2:
                    w.append(("dv", scans[g - 2]))
                a = op("scalar", lambda e, g=g, ab=ab: e.activation(
                    out=ASL[:, (g % 2) * TT:(g % 2 + 1) * TT], in_=ab[:, :], func=AF.Sigmoid,
                    bias=CV[:, C_WAB + g:C_WAB + g + 1]), inc="ac", waits=w)
                aev.append(a)

                ini = 0.0 if t == 0 else CAR[:, g:g + 1]
                w = [("ac", a), ("pe", pu)]
                if g == 0 and t > 0:
                    w.append(("dv", carries_dv[0]))
                s = op("vector", lambda e, g=g, ub=ub, ini=ini: e.tensor_tensor_scan(
                    out=hb(g), data0=ASL[:, (g % 2) * TT:(g % 2 + 1) * TT], data1=ub[:, :],
                    initial=ini, op0=ALU.mult, op1=ALU.add), inc="dv", waits=w)
                scans.append(s)

            act_prefetch(AF.Ln)
            carries = []
            for g in range(GD):
                s = op("vector", lambda e, g=g: e.tensor_copy(out=CAR[:, g:g + 1],
                                                              in_=hb(g)[:, TT - 1:TT]),
                       inc="dv", waits=[("dv", scans[-1])] if g == 0 else [])
                carries.append(s)
            carries_dv[0] = carries[-1]

            if t == 0 and "hb" in _DEBUG_TAPS:
                tap("hb", lambda: BFP[:, 8192:12288], [P, 4096], BF16)
            if t + 1 < NT:
                ldx[t + 1] = op("sync", lambda e, t=t, XO=XO: e.dma_start(
                    out=XO[:, :],
                    in_=xh.ap().rearrange("(g p) t -> p g t", p=P)[:, :, (t + 1) * TT:(t + 2) * TT]),
                    inc="ld", waits=[("dv", carries[-1])])

            ymm, y0_dv, sy1, sy2 = [], [], [], []
            for mi in range(GD + 3):
                if mi < GD:
                    m = mi
                    yb_ = PB[m % 4]

                    def ymmf(e, m=m, yb_=yb_):
                        for k in range(GD):
                            e.matmul(yb_[:, :],
                                     WC[:, k * 1024 + m * P: k * 1024 + (m + 1) * P],
                                     hb(k), start=(k == 0), stop=False)
                        return e.matmul(yb_[:, :], UBT[32:33, m * P:(m + 1) * P],
                                        ONESR[32:33, :], start=False, stop=True)
                    w = []
                    if m == 0:
                        w.append(("dv", scans[-1]))
                        if t == 0:
                            w.append(("dv", wc_ready_dv))
                        else:
                            w.append(ldwc[t])
                    y0w = [st_store[0]] if m == 0 else []
                    if m >= 4:
                        w.append(("dv", y0_dv[m - 4]))
                    pm_ = op("tensor", ymmf, inc="pe", waits=w)
                    ymm.append(pm_)
                    c = op("vector", lambda e, m=m, yb_=yb_, X=X: e.scalar_tensor_tensor(
                        out=fcs(m), in0=X[:, m * TT:(m + 1) * TT],
                        scalar=CV[:, C_D + m:C_D + m + 1], in1=yb_[:, :],
                        op0=ALU.mult, op1=ALU.add), inc="dv", waits=[("pe", pm_)] + y0w)
                    y0_dv.append(c)
                if mi >= 3:
                    j = mi - 3
                    c2 = op("scalar", lambda e, j=j: e.activation(
                        out=scrA(j), in_=fcs(j), func=AF.Copy), inc="ac",
                        waits=[("pe", scr_pe[0] if j < 2 else sy1[j - 2]),
                               ("dv", y0_dv[j])])
                    a2 = op("scalar", lambda e, j=j: e.activation(
                        out=scrB(j), in_=fcs(j), func=AF.Square), inc="ac",
                        waits=[("pe", scr_pe[1] if j < 2 else sy2[j - 2])])
                    m1 = op("tensor", lambda e, j=j: e.matmul(PB[6][:, :], ONES[:, :], scrA(j),
                                                              start=(j == 0), stop=(j == GD - 1)),
                            inc="pe", waits=[("ac", c2), ("dv", sums67_dv[0])])
                    sy1.append(m1)
                    m2 = op("tensor", lambda e, j=j: e.matmul(PB[7][:, :], ONES[:, :], scrB(j),
                                                              start=(j == 0), stop=(j == GD - 1)),
                            inc="pe", waits=[("ac", a2), ("dv", sums67_dv[0])])
                    sy2.append(m2)
            scr_pe[0] = sy1[-1]
            scr_pe[1] = sy2[-1]

            if t == 0 and "y0" in _DEBUG_TAPS:
                tap("y0", lambda: FC[:, :], [P, 4096], F32)
            op("vector", lambda e: e.engine_nop(), waits=[("pe", sy2[-1])])
            sums_c7, ready7 = stats_single(PB[6], PB[7], ONE_D, MCS, RCS, TMS, None)
            sums67_dv[0] = sums_c7

            x1_done = []
            firsts = [True]
            for g in range(GD):
                w = list(ready7) if firsts[0] else []
                firsts[0] = False
                op("vector", lambda e, g=g: e.tensor_tensor(
                    out=fcs(g), in0=fcs(g), in1=MCS[:, :], op=ALU.subtract), waits=w)
                op("vector", lambda e, g=g: e.tensor_tensor(
                    out=fcs(g), in0=fcs(g), in1=RCS[:, :], op=ALU.mult))
                p3 = op("vector", lambda e, g=g, X=X: e.tensor_tensor(
                    out=X[:, g * TT:(g + 1) * TT], in0=X[:, g * TT:(g + 1) * TT],
                    in1=fcs(g), op=ALU.add), inc="dv")
                x1_done.append(("dv", p3))

            if t == 0 and "x1" in _DEBUG_TAPS:
                tap("x1", lambda X=X: X[:, :], [P, 4096], F32)
            sx2 = []
            for g in range(GD):
                w = [x1_done[g], ("pe", scr_pe[1] if g < 2 else sx2[g - 2])]
                a = op("scalar", lambda e, g=g, X=X: e.activation(
                    out=scrB(g), in_=X[:, g * TT:(g + 1) * TT], func=AF.Square),
                    inc="ac", waits=w)
                m2 = op("tensor", lambda e, g=g: e.matmul(PB[7][:, :], ONES[:, :], scrB(g),
                                                          start=(g == 0), stop=(g == GD - 1)),
                        inc="pe", waits=[("ac", a), ("dv", sums_c7)])
                sx2.append(m2)
            scr_pe[1] = sx2[-1]
            op("vector", lambda e: e.engine_nop(), waits=[("pe", sx2[-1])])
            sums_c9, ready9 = stats_double(None, PB[7], ONE_D, M1S, RCS, TMS, MCS, True)
            sums67_dv[0] = sums_c9

            xn2_dv = norm_chain(X, C_L1G, C_L1B, M1S, RCS, ready9, [])
            if t == 0 and "xn2" in _DEBUG_TAPS:
                tap("xn2", lambda: BFP[:, 16384:20480], [P, 4096], BF16)

            abc_r1 = None
            if t + 1 < NT:
                s2last = abc_sums(t + 1)
                abc_r1 = abc_stats(t + 1, s2last)
            act_prefetch(AF.Gelu)

            mm1, gel, g2c, sg1, sg2, gg_dv = [], [], [], [], [], []
            for mi in range(GH + 2):
                if mi >= 2:
                    j = mi - 2
                    m1_ = op("tensor", lambda e, j=j: e.matmul(PB[6][:, :], ONES[:, :], gg(j),
                                                               start=(j == 0), stop=(j == GH - 1)),
                             inc="pe",
                             waits=[("ac", gel[j])] + ([("dv", sums_c9)] if j == 0 else []))
                    sg1.append(m1_)
                    m2_ = op("tensor", lambda e, j=j: e.matmul(PB[7][:, :], ONES[:, :], scrB(j),
                                                               start=(j == 0), stop=(j == GH - 1)),
                             inc="pe", waits=[("dv", g2c[j])])
                    sg2.append(m2_)
                    gc = op("scalar", lambda e, j=j: e.activation(
                        out=gg(j), in_=gg(j), func=AF.Identity,
                        scale=CV[:, C_L2G + j:C_L2G + j + 1]), inc="ac",
                        waits=[("pe", m1_)])
                    gg_dv.append(gc)
                if mi < GH:
                    m = mi
                    bk = PB[m % 4]

                    def m1f(e, m=m, bk=bk):
                        last = None
                        for k in range(GD):
                            last = e.matmul(bk[:, :],
                                            W1Q[:, k * 4096 + m * P: k * 4096 + (m + 1) * P],
                                            xn(k), start=(k == 0), stop=(k == GD - 1))
                        return last
                    w = [("ac", xn2_dv)] if m == 0 else []
                    if m >= 4:
                        w.append(("ac", gel[m - 4]))
                    pm_ = op("tensor", m1f, inc="pe", waits=w)
                    mm1.append(pm_)
                    a = op("scalar", lambda e, m=m, bk=bk: e.activation(
                        out=gg(m), in_=bk[:, :], func=AF.Gelu,
                        bias=CV[:, C_B1 + m:C_B1 + m + 1], scale=DRV[:, 1:2]), inc="ac",
                        waits=[("pe", pm_)])
                    gel.append(a)
                    w = [("ac", a), ("pe", scr_pe[1] if m < 2 else sg2[m - 2])]
                    c = op("vector", lambda e, m=m: e.tensor_tensor(out=scrB(m), in0=gg(m),
                                                                    in1=gg(m), op=ALU.mult),
                           inc="dv", waits=w)
                    g2c.append(c)
            scr_pe[1] = sg2[-1]
            act_prefetch(AF.Ln)

            if t == 0 and "g" in _DEBUG_TAPS:
                tap("g", lambda: BFP[:, 0:16384], [P, 16384], BF16)
            op("vector", lambda e: e.engine_nop(), waits=[("pe", sg2[-1])])
            sums_c12, ready12 = stats_single(PB[6], PB[7], ONE_H, MCS, RCS, TMS, None)
            sums67_dv[0] = sums_c12

            if t + 1 < NT:
                abc_chain(t + 1, abc_r1, [("pe", mm1[-1])])
            mr12 = op("gpsimd", lambda e: e.tensor_tensor(out=MRS[:, :], in0=MCS[:, :],
                                                          in1=RCS[:, :], op=ALU.mult),
                      inc="gp", waits=list(ready12))
            mr12w_ = [("gp", mr12)]

            mm2, e1_dv, e2_dv = [], [], []
            for m in range(GD):
                bk = PB[m % 4]

                def m2f(e, m=m, bk=bk):
                    last = None
                    for j in range(GH):
                        last = e.matmul(bk[:, :], W2Q[:, j * 1024 + m * P: j * 1024 + (m + 1) * P],
                                        gg(j), start=(j == 0), stop=(j == GH - 1))
                    return last
                w = [("ac", gg_dv[-1])] if m == 0 else []
                if m >= 4:
                    w.append(("dv", e1_dv[m - 4]))
                pm_ = op("tensor", m2f, inc="pe", waits=w)
                mm2.append(pm_)
                w = [("pe", pm_)]
                if m == 0:
                    w = w + ready12 + [("dv", kl_done[-1])]
                e1c = op("vector", lambda e, m=m, bk=bk: e.scalar_tensor_tensor(
                    out=fcs(m), in0=bk[:, :], scalar=DRV[:, 3:4], in1=RCS[:, :],
                    op0=ALU.mult, op1=ALU.mult), inc="dv", waits=w)
                e1_dv.append(e1c)
                w2_ = mr12w_ if m == 0 else []
                c = op("vector", lambda e, m=m: e.scalar_tensor_tensor(
                    out=fcs(m), in0=MRS[:, :], scalar=KK[:, m:m + 1], in1=fcs(m),
                    op0=ALU.mult, op1=ALU.add), inc="dv", waits=w2_)
                e2_dv.append(c)

            if t == 0 and "t2" in _DEBUG_TAPS:
                tap("t2", lambda: FC[:, :], [P, 4096], F32)
            st1, st2 = [], []
            for m in range(GD):
                c = op("scalar", lambda e, m=m: e.activation(
                    out=scrA(m), in_=fcs(m), func=AF.Identity, bias=LL[:, m:m + 1]), inc="ac",
                    waits=[("dv", e2_dv[m]), ("pe", scr_pe[0] if m < 2 else st1[m - 2])])
                a = op("scalar", lambda e, m=m: e.activation(
                    out=scrB(m), in_=fcs(m), func=AF.Square, bias=LL[:, m:m + 1]),
                    inc="ac",
                    waits=[("dv", e2_dv[m]), ("pe", scr_pe[1] if m < 2 else st2[m - 2])])
                m1_ = op("tensor", lambda e, m=m: e.matmul(PB[6][:, :], ONES[:, :], scrA(m),
                                                           start=(m == 0), stop=(m == GD - 1)),
                         inc="pe", waits=[("ac", c), ("dv", sums_c12)])
                st1.append(m1_)
                m2_ = op("tensor", lambda e, m=m: e.matmul(PB[7][:, :], ONES[:, :], scrB(m),
                                                           start=(m == 0), stop=(m == GD - 1)),
                         inc="pe", waits=[("ac", a), ("dv", sums_c12)])
                st2.append(m2_)
            scr_pe[0] = st1[-1]
            scr_pe[1] = st2[-1]

            op("vector", lambda e: e.engine_nop(), waits=[("pe", st2[-1])])
            sums_c14, ready14 = stats_single(PB[6], PB[7], ONE_D, MCS, RCS, TMS, None)
            sums67_dv[0] = sums_c14
            if t + 1 < NT:
                act_prefetch(AF.Sigmoid)

            out_done = []
            fdv = [True]
            fgp = [True]
            for g in range(GD):
                w = list(ready14) if fdv[0] else []
                fdv[0] = False
                p1 = op("vector", lambda e, g=g: e.scalar_tensor_tensor(
                    out=fcs(g), in0=fcs(g), scalar=LL[:, g:g + 1], in1=MCS[:, :],
                    op0=ALU.add, op1=ALU.subtract), inc="dv", waits=w)
                w2 = [("dv", p1)] + (list(ready14) if fgp[0] else [])
                fgp[0] = False
                op("gpsimd", lambda e, g=g: e.tensor_tensor(
                    out=fcs(g), in0=fcs(g), in1=RCS[:, :], op=ALU.mult), waits=w2)
                p3 = op("gpsimd", lambda e, g=g, X=X: e.tensor_tensor(
                    out=fcs(g), in0=X[:, g * TT:(g + 1) * TT], in1=fcs(g), op=ALU.add),
                    inc="gp")
                out_done.append(("gp", p3))

            so = op("sync", lambda e, t=t: e.dma_start(
                out=outh.ap().rearrange("(g p) t -> p g t", p=P)[:, :, t * TT:(t + 1) * TT],
                in_=FC[:, :]), inc="st",
                waits=[out_done[-1]])
            st_store[0] = so

            if t + 1 < NT:
                w_ldwc = [("pe", mm2[-1])] + ([wcs_st] if t == 0 else [])
                ldwc[t + 1] = op("sync", lambda e: e.dma_start(out=WC[:, 0:8192], in_=wcs[:, :]),
                                 inc="ld", waits=w_ldwc)

        fin = tuple((ln, cnt[ln]) for ln in ("st0", "st1") if cnt.get(ln))
        plan["sync"].append((lambda e: e.nop(), None, fin))

        def emit(eng_name):
            def _(e):
                for fn, inc, waits in plan[eng_name]:
                    for (s, v) in waits:
                        e.wait_ge(sem[s], v)
                    instr = fn(e)
                    if inc is not None:
                        instr.then_inc(sem[inc], 16 if inc.startswith(("ld", "st")) else 1)
            return _

        block.sync(emit("sync"))
        block.tensor(emit("tensor"))
        block.vector(emit("vector"))
        block.scalar(emit("scalar"))
        block.gpsimd(emit("gpsimd"))

    return nc


_CACHE = {}


def _pack_cv(d):
    cv = np.zeros((P, NCV), np.float32)

    def put(col, v, g):
        cv[:, col:col + g] = np.asarray(v, np.float32).reshape(g, P).T
    put(C_WAB, d["L0_attn_W_alpha_b"], 8)
    put(C_WCB, d["L0_attn_W_C_b"], 8)
    put(C_D, d["L0_attn_D_param"], 8)
    put(C_LNW, d["L0_attn_ln_w"], 8)
    put(C_LNB, d["L0_attn_ln_b"], 8)
    put(C_L1G, d["L0_ffn_ln1_g"], 8)
    put(C_L1B, d["L0_ffn_ln1_b"], 8)
    put(C_B2, d["L0_ffn_b2"], 8)
    put(C_B1, d["L0_ffn_b1"], 32)
    put(C_L2G, d["L0_ffn_ln2_g"], 32)
    put(C_L2B, d["L0_ffn_ln2_b"], 32)
    return cv


def kernel(**inputs):
    d = {k: np.asarray(v, dtype=np.float32) for k, v in inputs.items()}
    x = d["L0_attn_x"]
    waT = np.ascontiguousarray(d["L0_attn_W_alpha_w"].T)
    wbT = np.ascontiguousarray(d["L0_attn_W_B_w"].T)
    wcT = np.ascontiguousarray(d["L0_attn_W_C_w"].T)
    w1T = np.ascontiguousarray(d["L0_ffn_w1"].T)
    w2T = np.ascontiguousarray(d["L0_ffn_w2"].T)
    cv = _pack_cv(d)
    wbb = np.zeros((33, D), np.float32)
    wbb[0] = d["L0_attn_W_B_b"]
    wbb[32] = d["L0_attn_W_C_b"]

    if "nc" not in _CACHE:
        _CACHE["nc"] = build_program()
    nc = _CACHE["nc"]

    in_maps = [{
        "x": np.ascontiguousarray(x[c].T),
        "waT": waT, "wbT": wbT, "wcT": wcT, "w1T": w1T, "w2T": w2T,
        "cvec": cv, "brow": wbb,
    } for c in range(8)]
    res = run_bass_kernel_spmd(nc, in_maps, core_ids=list(range(8)))
    out = np.stack([np.asarray(r["out"]).T for r in res.results], axis=0)
    return out.astype(np.float32)
